# version 05
# baseline (speedup 1.0000x reference)
import numpy as np
import ml_dtypes
from contextlib import ExitStack

import concourse.bass as bass
import concourse.mybir as mybir
from concourse.bass_utils import run_bass_kernel_spmd

F32 = mybir.dt.float32
BF16 = mybir.dt.bfloat16
AF = mybir.ActivationFunctionType
ALU = mybir.AluOpType
AX = mybir.AxisListType
bfloat16 = ml_dtypes.bfloat16

D = 4096
S = 4096
TOK = 1024
NH = 32
NI = 11008
NF = NI // 128
NC = D // 128
NQR = 4
NT = 8
RSCALE = 0.08838834764831845

ENGS = ("sync", "tensor", "scalar", "vector", "gpsimd")


class Prog:
    def __init__(self, counts):
        self.ops = {e: [] for e in ENGS}
        self.n = counts

    def cnt(self, sem):
        return self.n.get(sem, 0)

    def op(self, eng, fn, waits=(), inc=None, incs=()):
        ivs = list(incs)
        if inc is not None:
            ivs.append(inc)
        for sem, k in ivs:
            self.n[sem] = self.n.get(sem, 0) + k
        self.ops[eng].append((tuple(waits), fn, tuple(ivs)))


def emit(nc, prog, sems):
    with nc.Block() as blk:
        def mk(name):
            ops = prog.ops[name]
            def run(eng):
                for waits, fn, ivs in ops:
                    for sem, val in waits:
                        eng.wait_ge(sems[sem], val)
                    inst = fn(eng)
                    for sem, k in ivs:
                        inst = inst.then_inc(sems[sem], k)
            return run
        blk.sync(mk("sync"))
        blk.tensor(mk("tensor"))
        blk.scalar(mk("scalar"))
        blk.vector(mk("vector"))
        blk.gpsimd(mk("gpsimd"))


def build(debug=False, phases=(1, 2, 3)):
    nc = bass.Bass("TRN2", target_bir_lowering=False)

    x_d = nc.dram_tensor("x", [S, D], F32, kind="ExternalInput")
    wq_d = nc.dram_tensor("wq", [NH, 128, D], BF16, kind="ExternalInput")
    wk_d = nc.dram_tensor("wk", [NC, 128, D], BF16, kind="ExternalInput")
    wv_d = nc.dram_tensor("wv", [D, D], BF16, kind="ExternalInput")
    bq_d = nc.dram_tensor("bq", [128, NH], F32, kind="ExternalInput")
    bk_d = nc.dram_tensor("bk", [128, NC], F32, kind="ExternalInput")
    bv_d = nc.dram_tensor("bv", [1, D], BF16, kind="ExternalInput")
    gw_d = nc.dram_tensor("gw", [NF, 128, D], BF16, kind="ExternalInput")
    uw_d = nc.dram_tensor("uw", [NF, 128, D], BF16, kind="ExternalInput")
    dw_d = nc.dram_tensor("dw", [8, NF, 128, 512], BF16, kind="ExternalInput")
    gs_d = nc.dram_tensor("gs", [1, 1], F32, kind="ExternalInput")
    us_d = nc.dram_tensor("us", [1, 1], F32, kind="ExternalInput")
    ds_d = nc.dram_tensor("ds", [1, 1], F32, kind="ExternalInput")
    out_d = nc.dram_tensor("out", [TOK, D], F32, kind="ExternalOutput")

    dbg_kind = "ExternalOutput" if debug else "Internal"
    kt_all = nc.dram_tensor("kt_all", [D, S], BF16, kind=dbg_kind)
    v_all = nc.dram_tensor("v_all", [S, D], BF16, kind=dbg_kind)
    ffn_tm = nc.dram_tensor("ffn_tm", [TOK, D], F32, kind=dbg_kind)
    if debug:
        xn_dbg = nc.dram_tensor("xn_dbg", [128, NC * TOK], BF16, kind="ExternalOutput")

    ident_d = nc.inline_tensor(np.eye(128, dtype=bfloat16), name="ident")

    counts = {}
    sem_names = [
        "ph",
        "dX0", "dX1", "dWk0", "dWk1", "dWv0", "dWv1", "dWv2", "dWv3", "dWg0", "dWg1", "dWu0", "dWu1", "dWd0", "dWd1", "dWd2", "dWd3", "dKVi0", "dKVi1", "dKo0", "dKo1", "dVo0", "dVo1", "dEp0", "dEp1", "dOut0", "dOut1", "dC", "dFO",
        "sLNd", "sLNa", "sXNT", "sTP", "sXNC",
        "sKP", "sKE", "sVP", "sVS", "sVE",
        "sGP", "sUP", "sSI", "sGU", "sDP", "sDS", "sDE",
        "sQP", "sQE", "sKQ", "sNM", "sNR", "sBC",
        "sSC", "sEX", "sCT", "sEO", "sCE",
        "sP0",
    ]

    with ExitStack() as outer:
        sems = {s: outer.enter_context(nc.semaphore(s)) for s in sem_names}

        id_sb = outer.enter_context(nc.sbuf_tensor("id_sb", [128, 128], BF16))
        ones_row_bf = outer.enter_context(nc.sbuf_tensor("ones_row_bf", [1, 128], BF16))
        ones_row_f = outer.enter_context(nc.sbuf_tensor("ones_row_f", [1, 128], F32))
        ones_col_bf = outer.enter_context(nc.sbuf_tensor("ones_col_bf", [128, 1], BF16))
        bq_sb = outer.enter_context(nc.sbuf_tensor("bq_sb", [128, NH], F32))
        bk_sb = outer.enter_context(nc.sbuf_tensor("bk_sb", [128, NC], F32))
        bv_sb = outer.enter_context(nc.sbuf_tensor("bv_sb", [1, D], BF16))
        gs_sb = outer.enter_context(nc.sbuf_tensor("gs_sb", [1, 1], F32))
        us_sb = outer.enter_context(nc.sbuf_tensor("us_sb", [1, 1], F32))
        ds_sb = outer.enter_context(nc.sbuf_tensor("ds_sb", [1, 1], F32))
        usds1 = outer.enter_context(nc.sbuf_tensor("usds1", [1, 1], F32))
        gs_bc = outer.enter_context(nc.sbuf_tensor("gs_bc", [128, 1], F32))
        usds_bc = outer.enter_context(nc.sbuf_tensor("usds_bc", [128, 1], F32))
        eps_sb = outer.enter_context(nc.sbuf_tensor("eps_sb", [128, 1], F32))
        scr1 = outer.enter_context(nc.sbuf_tensor("scr1", [1, 2], F32))
        xn_own = outer.enter_context(nc.sbuf_tensor("xn_own", [128, NC * TOK], BF16))

        def barrier(p, fence_ap):
            base = p.cnt("ph")
            p.op("sync", lambda g: g.nop(), inc=("ph", 1))
            p.op("gpsimd", lambda g: g.nop(), inc=("ph", 1))
            p.op("tensor", lambda g, fp=fence_ap: g.matmul(
                fp, ones_row_bf[:], ones_row_bf[:], start=True, stop=True))
            p.op("tensor", lambda g, fp=fence_ap: g.matmul(
                fp, ones_row_bf[:], ones_row_bf[:], start=True, stop=True), inc=("ph", 1))
            p.op("vector", lambda g: g.memset(scr1[0:1, 0:1], 0.0), inc=("ph", 1))
            p.op("scalar", lambda g: g.activation(scr1[0:1, 1:2], gs_sb[0:1, 0:1], AF.Square),
                 waits=[("dC", p.cnt("dC"))], inc=("ph", 1))
            tgt = base + 5
            for e2 in ENGS:
                p.op(e2, lambda g: g.nop(), waits=[("ph", tgt)])

        p0 = Prog(counts)
        with ExitStack() as pctx:
            ps0 = pctx.enter_context(nc.psum_tensor("ps0", [128, 2], F32))
            ps0b = pctx.enter_context(nc.psum_tensor("ps0b", [128, 128], F32))
            for dst, src in [(id_sb, ident_d), (bq_sb, bq_d), (bk_sb, bk_d), (bv_sb, bv_d),
                             (gs_sb, gs_d), (us_sb, us_d), (ds_sb, ds_d)]:
                p0.op("gpsimd", lambda g, dst=dst, src=src: g.dma_start(out=dst[:], in_=src[:, :]),
                      inc=("dC", 16))
            dc_all = p0.cnt("dC")
            p0.op("vector", lambda g: g.memset(ones_row_bf[:], 1.0), inc=("sP0", 1))
            p0.op("vector", lambda g: g.memset(ones_row_f[:], 1.0), inc=("sP0", 1))
            p0.op("vector", lambda g: g.memset(ones_col_bf[:], 1.0), inc=("sP0", 1))
            p0.op("vector", lambda g: g.memset(eps_sb[:], 1e-5), inc=("sP0", 1))
            p0.op("vector", lambda g: g.tensor_mul(usds1[:], us_sb[:], ds_sb[:]),
                  waits=[("dC", dc_all), ("sP0", 4)], inc=("sP0", 1))
            p0.op("tensor", lambda g: g.matmul(ps0[:, 0:1], ones_row_f[:], gs_sb[:], start=True, stop=True),
                  waits=[("sP0", 5), ("dC", dc_all)])
            p0.op("tensor", lambda g: g.matmul(ps0[:, 1:2], ones_row_f[:], usds1[:], start=True, stop=True))
            p0.op("tensor", lambda g: g.matmul(ps0b[:, 0:128], ones_row_bf[:], ones_row_bf[:],
                                               start=True, stop=True), inc=("sP0", 1))
            p0.op("vector", lambda g: g.tensor_copy(gs_bc[:], ps0[:, 0:1]), waits=[("sP0", 6)])
            p0.op("vector", lambda g: g.tensor_copy(usds_bc[:], ps0[:, 1:2]), inc=("sP0", 1))
            p0.op("gpsimd", lambda g: g.nop(), waits=[("dC", dc_all)])
            barrier(p0, ps0b[:, 0:128])
            emit(nc, p0, sems)

        if 1 in phases:
            p1 = Prog(counts)
            with ExitStack() as pctx:
                e = pctx.enter_context
                x_sb = [e(nc.sbuf_tensor(f"x_sb{i}", [128, D], F32)) for i in range(2)]
                xn_tmp = e(nc.sbuf_tensor("xn_tmp", [128, NC * TOK], BF16))
                xn_tm = e(nc.sbuf_tensor("xn_tm", [128, D], BF16))
                stat = e(nc.sbuf_tensor("stat", [128, 8], F32))
                wk_sb = [e(nc.sbuf_tensor(f"wk_sb{i}", [128, D], BF16)) for i in range(2)]
                wv_sb = [e(nc.sbuf_tensor(f"wv_sb{i}", [128, 512], BF16)) for i in range(4)]
                k_ev = e(nc.sbuf_tensor("k_ev", [128, TOK], BF16))
                v_ev = [e(nc.sbuf_tensor(f"v_ev{i}", [128, 4 * 512], BF16)) for i in range(2)]
                psA = e(nc.psum_tensor("psA", [128, 1024], F32))
                psB = e(nc.psum_tensor("psB", [128, 1024], F32))
                tpp = [e(nc.psum_tensor(f"tp{i}", [128, 1024], BF16)) for i in range(2)]
                tpF = e(nc.psum_tensor("tpF", [128, 128], F32))

                def fence1(sem):
                    p1.op("tensor", lambda g: g.matmul(
                        tpF[:, 0:128], ones_row_bf[:], ones_row_bf[:], start=True, stop=True),
                        inc=(sem, 1))

                n_xtile = 0
                n_group_v = 0
                kst = {}
                vst = {}
                for qr in range(NQR):
                    xn_fm = xn_own if qr == 0 else xn_tmp
                    for ti in range(NT):
                        xb = x_sb[n_xtile % 2]
                        r0 = qr * TOK + ti * 128
                        dxs = f"dX{n_xtile % 2}"
                        w = [("sXNT", n_xtile - 1)] if n_xtile >= 2 else []
                        p1.op("gpsimd", lambda g, xb=xb, r0=r0: g.dma_start(
                            out=xb[:], in_=x_d[r0:r0 + 128, :]), waits=w, inc=(dxs, 16))
                        ld = p1.cnt(dxs)
                        p1.op("vector", lambda g, xb=xb: g.reduce_sum(
                            stat[:, 0:1], xb[:], axis=AX.X),
                            waits=[(dxs, ld), ("sLNd", p1.cnt("sLNd"))], inc=("sLNd", 1))
                        p1.op("scalar", lambda g, xb=xb: g.activation(
                            xn_tm[:], xb[:], AF.Square, accum_out=stat[:, 1:2]),
                            waits=[(dxs, ld), ("sTP", 8 * n_xtile)], inc=("sLNa", 1))
                        a0 = p1.cnt("sLNa")
                        d0 = p1.cnt("sLNd")
                        p1.op("vector", lambda g: g.tensor_scalar(
                            stat[:, 2:3], stat[:, 0:1], 1.0 / D, None, ALU.mult),
                            waits=[("sLNd", d0)], inc=("sLNd", 1))
                        p1.op("vector", lambda g: g.tensor_scalar(
                            stat[:, 6:7], stat[:, 1:2], 1.0 / D, None, ALU.mult),
                            waits=[("sLNa", a0), ("sLNd", p1.cnt("sLNd"))], inc=("sLNd", 1))
                        p1.op("vector", lambda g: g.scalar_tensor_tensor(
                            stat[:, 3:4], stat[:, 2:3], -1.0, stat[:, 2:3], ALU.mult, ALU.mult),
                            waits=[("sLNd", p1.cnt("sLNd"))], inc=("sLNd", 1))
                        p1.op("vector", lambda g: g.tensor_add(
                            stat[:, 3:4], stat[:, 3:4], stat[:, 6:7]),
                            waits=[("sLNd", p1.cnt("sLNd"))], inc=("sLNd", 1))
                        d1 = p1.cnt("sLNd")
                        p1.op("scalar", lambda g: g.activation(
                            stat[:, 7:8], stat[:, 3:4], AF.Sqrt, bias=eps_sb[:], scale=1.0),
                            waits=[("sLNd", d1)], inc=("sLNa", 1))
                        a1 = p1.cnt("sLNa")
                        p1.op("vector", lambda g: g.reciprocal(
                            stat[:, 4:5], stat[:, 7:8]),
                            waits=[("sLNa", a1), ("sLNd", p1.cnt("sLNd"))], inc=("sLNd", 1))
                        p1.op("vector", lambda g: g.tensor_scalar(
                            stat[:, 5:6], stat[:, 2:3], stat[:, 4:5], -1.0, ALU.mult, ALU.mult),
                            waits=[("sLNd", p1.cnt("sLNd"))], inc=("sLNd", 1))
                        p1.op("vector", lambda g, xb=xb: g.tensor_scalar(
                            xn_tm[:], xb[:], stat[:, 4:5], stat[:, 5:6], ALU.mult, ALU.add),
                            waits=[("sLNd", p1.cnt("sLNd"))], inc=("sXNT", 1))
                        xnt = p1.cnt("sXNT")
                        for grp in range(8):
                            tp = tpp[grp % 2]
                            gidx = n_xtile * 8 + grp
                            w = [("sXNT", xnt)]
                            if gidx >= 2:
                                w.append(("sXNC", (gidx - 1) * 4))
                            for j4 in range(4):
                                ic = grp * 4 + j4
                                p1.op("tensor", lambda g, tp=tp, j4=j4, ic=ic: g.transpose(
                                    tp[:, j4 * 128:(j4 + 1) * 128], xn_tm[:, ic * 128:(ic + 1) * 128], id_sb[:]),
                                    waits=w if j4 == 0 else (),
                                    inc=("sTP", 1) if (j4 == 0 and grp >= 1) else None)
                            tpc_fence = p1.cnt("sTP") + 1
                            for j4 in range(4):
                                ic = grp * 4 + j4
                                col = ic * TOK + ti * 128
                                p1.op("vector", lambda g, tp=tp, j4=j4, xn_fm=xn_fm, col=col: g.tensor_copy(
                                    xn_fm[:, col:col + 128], tp[:, j4 * 128:(j4 + 1) * 128]),
                                    waits=[("sTP", tpc_fence)] if j4 == 0 else (),
                                    inc=("sXNC", 1))
                        fence1("sTP")
                        n_xtile += 1
                    xnc_full = p1.cnt("sXNC")

                    for oc in range(NC):
                        wkb = wk_sb[oc % 2]
                        gi = qr * NC + oc
                        wks = f"dWk{oc % 2}"
                        w = [("sKP", gi - 1)] if gi >= 2 else []
                        p1.op("sync", lambda g, wkb=wkb, oc=oc: g.dma_start(
                            out=wkb[:], in_=wk_d[oc]), waits=w, inc=(wks, 16))
                        ldw = p1.cnt(wks)
                        acc = psA if oc % 2 == 0 else psB
                        w = [(wks, ldw), ("sXNC", xnc_full)]
                        if gi >= 2:
                            w.append(("sKE", gi - 1))
                        if qr >= 1 and oc < 2:
                            w.append(("sVE", 64 * qr))
                        for ic in range(NC):
                            for nb in range(2):
                                first = (nb == 0 and ic == 0)
                                p1.op("tensor", lambda g, acc=acc, nb=nb, ic=ic, wkb=wkb, xn_fm=xn_fm: g.matmul(
                                    acc[:, nb * 512:(nb + 1) * 512],
                                    wkb[:, ic * 128:(ic + 1) * 128],
                                    xn_fm[:, ic * TOK + nb * 512: ic * TOK + (nb + 1) * 512],
                                    start=(ic == 0), stop=(ic == NC - 1)),
                                    waits=w if first else (),
                                    inc=("sKP", 1) if (first and oc >= 1) else None)
                        if oc == NC - 1:
                            fence1("sKP")
                        kb = k_ev
                        kos = "dKo0"
                        w = [("sKP", gi + 1)]
                        if gi >= 1:
                            w.append((kos, kst[gi - 1]))
                        p1.op("scalar", lambda g, kb=kb, acc=acc, oc=oc: g.activation(
                            kb[:], acc[:], AF.Identity, bias=bk_sb[:, oc:oc + 1], scale=1.0),
                            waits=w, inc=("sKE", 1))
                        ke = p1.cnt("sKE")
                        p1.op("gpsimd", lambda g, kb=kb, oc=oc, qr=qr: g.dma_start(
                            out=kt_all[oc * 128:(oc + 1) * 128, qr * TOK:(qr + 1) * TOK], in_=kb[:]),
                            waits=[("sKE", ke)], inc=(kos, 16))
                        kst[gi] = p1.cnt(kos)

                    for ttg in range(2):
                        for db in range(8):
                            gi = n_group_v
                            accs = [psA[:, 0:512], psA[:, 512:1024], psB[:, 0:512], psB[:, 512:1024]]
                            for ic in range(NC):
                                wvb = wv_sb[ic % 4]
                                gsl = gi * NC + ic
                                wvs = f"dWv{ic % 4}"
                                w = [("sVS", gsl - 3)] if gsl >= 4 else []
                                p1.op("sync", lambda g, wvb=wvb, ic=ic, db=db: g.dma_start(
                                    out=wvb[:], in_=wv_d[ic * 128:(ic + 1) * 128, db * 512:(db + 1) * 512]),
                                    waits=w, inc=(wvs, 16))
                                ldv = p1.cnt(wvs)
                                for tt in range(4):
                                    t_idx = ttg * 4 + tt
                                    w2 = [(wvs, ldv)] if tt == 0 else []
                                    if ic == 0:
                                        if gi % 16 == 0:
                                            if tt == 0:
                                                w2.append(("sKE", p1.cnt("sKE")))
                                        else:
                                            w2.append(("sVE", 4 * (gi - 1) + tt + 1))
                                    p1.op("tensor", lambda g, accs=accs, tt=tt, ic=ic, t_idx=t_idx, wvb=wvb, xn_fm=xn_fm: g.matmul(
                                        accs[tt], xn_fm[:, ic * TOK + t_idx * 128: ic * TOK + t_idx * 128 + 128],
                                        wvb[:], start=(ic == 0), stop=False),
                                        waits=w2,
                                        inc=("sVS", 1) if (tt == 0 and ic >= 1) else None)
                            for tt in range(4):
                                p1.op("tensor", lambda g, accs=accs, tt=tt, db=db: g.matmul(
                                    accs[tt], ones_row_bf[:], bv_sb[:, db * 512:(db + 1) * 512],
                                    start=False, stop=True),
                                    inc=("sVS", 1) if tt == 0 else None)
                            fence1("sVP")
                            vb = v_ev[gi % 2]
                            vos = f"dVo{gi % 2}"
                            w = [("sVP", gi + 1)]
                            if gi >= 2:
                                w.append((vos, vst[gi - 2]))
                            for tt in range(4):
                                p1.op("scalar", lambda g, vb=vb, accs=accs, tt=tt: g.activation(
                                    vb[:, tt * 512:(tt + 1) * 512], accs[tt], AF.Copy),
                                    waits=w if tt == 0 else (), inc=("sVE", 1))
                            ve = p1.cnt("sVE")
                            for tt in range(4):
                                t_idx = ttg * 4 + tt
                                r0 = qr * TOK + t_idx * 128
                                p1.op("gpsimd", lambda g, vb=vb, tt=tt, r0=r0, db=db: g.dma_start(
                                    out=v_all[r0:r0 + 128, db * 512:(db + 1) * 512],
                                    in_=vb[:, tt * 512:(tt + 1) * 512]),
                                    waits=[("sVE", ve)] if tt == 0 else (), inc=(vos, 16))
                            vst[gi] = p1.cnt(vos)
                            n_group_v += 1

                if debug:
                    p1.op("gpsimd", lambda g: g.dma_start(out=xn_dbg[:, :], in_=xn_own[:, :]),
                          waits=[("sXNC", 256)], inc=("dFO", 16))
                p1.op("gpsimd", lambda g: g.nop(), waits=[
                    ("dX0", p1.cnt("dX0")), ("dX1", p1.cnt("dX1")),
                    ("dKo0", p1.cnt("dKo0")), ("dKo1", p1.cnt("dKo1")),
                    ("dVo0", p1.cnt("dVo0")), ("dVo1", p1.cnt("dVo1")),
                    ("dFO", p1.cnt("dFO"))])
                p1.op("sync", lambda g: g.nop(), waits=[
                    ("dWk0", p1.cnt("dWk0")), ("dWk1", p1.cnt("dWk1"))] + [
                    (f"dWv{i}", p1.cnt(f"dWv{i}")) for i in range(4)])
                barrier(p1, tpF[:, 0:128])
                emit(nc, p1, sems)

        if 2 in phases:
            p2 = Prog(counts)
            with ExitStack() as pctx:
                e = pctx.enter_context
                gw_sb = [e(nc.sbuf_tensor(f"gw_sb{i}", [128, D], BF16)) for i in range(2)]
                uw_sb = [e(nc.sbuf_tensor(f"uw_sb{i}", [128, D], BF16)) for i in range(2)]
                dw_sb = [e(nc.sbuf_tensor(f"dw_sb{i}", [128, 512], BF16)) for i in range(4)]
                silu_sb = [e(nc.sbuf_tensor(f"silu_sb{i}", [128, 512], BF16)) for i in range(2)]
                gu_sb = e(nc.sbuf_tensor("gu_sb", [128, NF * 512], BF16))
                f_ev = e(nc.sbuf_tensor("f_ev", [128, 4 * 512], F32))
                psg = [e(nc.psum_tensor(f"psg{i}", [128, 512], F32)) for i in range(2)]
                psu = [e(nc.psum_tensor(f"psu{i}", [128, 512], F32)) for i in range(2)]
                psd = [e(nc.psum_tensor(f"psd{i}", [128, 512], F32)) for i in range(4)]

                fo_cnt = {}
                for th in range(2):
                    t0 = th * 512
                    de_prev = p2.cnt("sDE")
                    for f in range(NF):
                        gi = th * NF + f
                        gwb, uwb = gw_sb[f % 2], uw_sb[f % 2]
                        wgs, wus = f"dWg{f % 2}", f"dWu{f % 2}"
                        w = [("sUP", gi - 1)] if gi >= 2 else []
                        p2.op("sync", lambda g, gwb=gwb, f=f: g.dma_start(out=gwb[:], in_=gw_d[f]),
                              waits=w, inc=(wgs, 16))
                        p2.op("sync", lambda g, uwb=uwb, f=f: g.dma_start(out=uwb[:], in_=uw_d[f]),
                              inc=(wus, 16))
                        ldg, ldu = p2.cnt(wgs), p2.cnt(wus)
                        pg, pu = psg[f % 2], psu[f % 2]
                        w = [(wgs, ldg)]
                        if gi >= 2:
                            w.append(("sSI", gi - 1))
                        if th >= 1 and f < 2:
                            w.append(("sDE", de_prev))
                        for ic in range(NC):
                            p2.op("tensor", lambda g, pg=pg, gwb=gwb, ic=ic, t0=t0: g.matmul(
                                pg[:], gwb[:, ic * 128:(ic + 1) * 128],
                                xn_own[:, ic * TOK + t0: ic * TOK + t0 + 512],
                                start=(ic == 0), stop=(ic == NC - 1)),
                                waits=w if ic == 0 else (),
                                inc=("sUP", 1) if (ic == 0 and f >= 1) else None)
                        w = [(wus, ldu)]
                        if gi >= 2:
                            w.append(("sGU", gi - 1))
                        if th >= 1 and f < 2:
                            w.append(("sDE", de_prev))
                        for ic in range(NC):
                            p2.op("tensor", lambda g, pu=pu, uwb=uwb, ic=ic, t0=t0: g.matmul(
                                pu[:], uwb[:, ic * 128:(ic + 1) * 128],
                                xn_own[:, ic * TOK + t0: ic * TOK + t0 + 512],
                                start=(ic == 0), stop=(ic == NC - 1)),
                                waits=w if ic == 0 else (),
                                inc=("sGP", 1) if ic == 0 else None)
                        sl = silu_sb[f % 2]
                        w = [("sGP", gi + 1)]
                        if gi >= 2:
                            w.append(("sGU", gi - 1))
                        p2.op("scalar", lambda g, sl=sl, pg=pg: g.activation(
                            sl[:], pg[:], AF.Silu, bias=0.0, scale=gs_bc[:, 0:1]),
                            waits=w, inc=("sSI", 1))
                        si = p2.cnt("sSI")
                        p2.op("vector", lambda g, sl=sl, pu=pu, f=f: g.tensor_tensor(
                            gu_sb[:, f * 512:(f + 1) * 512], sl[:], pu[:], ALU.mult),
                            waits=[("sSI", si), ("sUP", gi + 1)], inc=("sGU", 1))
                    p2.op("tensor", lambda g: g.matmul(
                        psd[0][0:128, 0:128], ones_row_bf[:], ones_row_bf[:], start=True, stop=True),
                        inc=("sUP", 1))
                    gu_all = (th + 1) * NF

                    for db in range(8):
                        gi = th * 8 + db
                        accs = [psd[0][:], psd[1][:], psd[2][:], psd[3][:]] if db % 2 == 0 else \
                               [psg[0][:], psg[1][:], psu[0][:], psu[1][:]]
                        for f in range(NF):
                            gf = gi * NF + f
                            dwb = dw_sb[gf % 4]
                            wds = f"dWd{gf % 4}"
                            w = [("sDS", gf - 3)] if gf >= 4 else []
                            p2.op("sync", lambda g, dwb=dwb, db=db, f=f: g.dma_start(
                                out=dwb[:], in_=dw_d[db, f]), waits=w, inc=(wds, 16))
                            ldd = p2.cnt(wds)
                            for tb in range(4):
                                w2 = [(wds, ldd)]
                                if f == 0 and tb == 0:
                                    w2.append(("sGU", gu_all))
                                    if gi >= 2:
                                        w2.append(("sDE", (gi - 2) * 4 + 4))
                                incs = []
                                if tb == 0 and gf >= 1:
                                    incs.append(("sDS", 1))
                                if tb == 1 and f == 0 and db >= 1:
                                    incs.append(("sDP", 1))
                                p2.op("tensor", lambda g, accs=accs, tb=tb, f=f, dwb=dwb: g.matmul(
                                    accs[tb], gu_sb[:, f * 512 + tb * 128: f * 512 + tb * 128 + 128],
                                    dwb[:], start=(f == 0), stop=(f == NF - 1)),
                                    waits=w2 if tb == 0 else (), incs=incs)
                        fb = f_ev
                        w = [("sDP", gi + 1)]
                        if gi >= 1:
                            w.append(("dFO", fo_cnt[gi - 1]))
                        for tb in range(4):
                            p2.op("vector", lambda g, fb=fb, accs=accs, tb=tb: g.tensor_scalar(
                                fb[:, tb * 512:(tb + 1) * 512], accs[tb], usds_bc[:, 0:1], None, ALU.mult),
                                waits=w if tb == 0 else (), inc=("sDE", 1))
                            de = p2.cnt("sDE")
                            r0 = t0 + tb * 128
                            p2.op("gpsimd", lambda g, fb=fb, tb=tb, r0=r0, db=db: g.dma_start(
                                out=ffn_tm[r0:r0 + 128, db * 512:(db + 1) * 512],
                                in_=fb[:, tb * 512:(tb + 1) * 512]),
                                waits=[("sDE", de)], inc=("dFO", 16))
                        fo_cnt[gi] = p2.cnt("dFO")
                    p2.op("tensor", lambda g: g.matmul(
                        psd[0][0:128, 0:128], ones_row_bf[:], ones_row_bf[:], start=True, stop=True),
                        waits=[("sDE", (th * 8 + 7) * 4)], inc=("sDP", 1))
                p2.op("tensor", lambda g: g.matmul(
                    psd[0][0:128, 0:128], ones_row_bf[:], ones_row_bf[:], start=True, stop=True),
                    waits=[("sDE", p2.cnt("sDE"))], inc=("sDS", 1))
                p2.op("tensor", lambda g: g.nop(), waits=[("sDE", p2.cnt("sDE"))])
                p2.op("gpsimd", lambda g: g.nop(), waits=[("dFO", p2.cnt("dFO"))])
                p2.op("sync", lambda g: g.nop(), waits=[
                    ("dWg0", p2.cnt("dWg0")), ("dWg1", p2.cnt("dWg1")),
                    ("dWu0", p2.cnt("dWu0")), ("dWu1", p2.cnt("dWu1"))] + [
                    (f"dWd{i}", p2.cnt(f"dWd{i}")) for i in range(4)])
                barrier(p2, psd[0][0:128, 0:128])
                emit(nc, p2, sems)

        if 3 in phases:
            p3 = Prog(counts)
            with ExitStack() as pctx:
                e = pctx.enter_context
                khs = [e(nc.sbuf_tensor(f"khs{i}", [128, S], BF16)) for i in range(2)]
                vhs = [e(nc.sbuf_tensor(f"vhs{i}", [128, 32 * 129], BF16)) for i in range(2)]
                wqs = [e(nc.sbuf_tensor(f"wqs{i}", [128, D], BF16)) for i in range(2)]
                q_sb = e(nc.sbuf_tensor("q_sb", [128, TOK], BF16))
                qsq_sb = e(nc.sbuf_tensor("qsq_sb", [128, TOK], BF16))
                ksq_sb = e(nc.sbuf_tensor("ksq_sb", [128, S], BF16))
                pt_sb = [e(nc.sbuf_tensor(f"pt_sb{i}", [128, 1024], BF16)) for i in range(2)]
                nrm = e(nc.sbuf_tensor("nrm", [1, 8], F32))
                bias_sb = e(nc.sbuf_tensor("bias_sb", [128, 1], F32))
                rden = e(nc.sbuf_tensor("rden", [128, 1], F32))
                x_t = [e(nc.sbuf_tensor(f"x_t{i}", [128, 8 * 128], F32)) for i in range(2)]
                f_t = [e(nc.sbuf_tensor(f"f_t{i}", [128, 8 * 128], F32)) for i in range(2)]
                xf_t = [e(nc.sbuf_tensor(f"xf_t{i}", [128, 128], F32)) for i in range(2)]
                o_sb = [e(nc.sbuf_tensor(f"o_sb{i}", [128, 128], F32)) for i in range(2)]
                ctx_sb = [e(nc.sbuf_tensor(f"ctx_sb{i}", [128, 516], F32)) for i in range(2)]
                ringA = e(nc.psum_tensor("ringA", [128, 1024], F32))
                ringB = e(nc.psum_tensor("ringB", [128, 1024], F32))
                psE = e(nc.psum_tensor("psE", [128, 512], F32))
                psF = e(nc.psum_tensor("psF", [128, 512], F32))
                psN = [e(nc.psum_tensor(f"psN{i}", [128, 512], F32)) for i in range(2)]

                def fence3(sem, waits=()):
                    p3.op("tensor", lambda g: g.matmul(
                        ringB[0:128, 0:128], ones_row_bf[:], ones_row_bf[:], start=True, stop=True),
                        waits=waits, inc=(sem, 1))

                for i in range(2):
                    p3.op("vector", lambda g, i=i: g.memset(
                        vhs[i][:, :].rearrange("p (c k) -> p c k", k=129)[:, :, 128:129], 1.0),
                        inc=("sP0", 1))
                ones_done = p3.cnt("sP0")

                n_ct = 0
                n_eo = 0
                prev_bias_ready = 0
                g_pair = 0
                exp_cnt = {}
                fence_cnt = {}
                for h in range(NH):
                    par = h % 2
                    kb, vb, wb = khs[par], vhs[par], wqs[par]
                    kvs = f"dKVi{h % 2}"
                    w = [("sCT", (h - 2) * 2 + 2)] if h >= 2 else [("sP0", ones_done)]
                    p3.op("sync", lambda g, kb=kb, h=h: g.dma_start(
                        out=kb[:], in_=kt_all[h * 128:(h + 1) * 128, :]), waits=w, inc=(kvs, 16))
                    for c8 in range(8):
                        p3.op("sync", lambda g, vb=vb, h=h, c8=c8: g.dma_start(
                            out=vb[:, c8 * 4 * 129:(c8 + 1) * 4 * 129].rearrange(
                                "p (c k) -> p c k", k=129)[:, :, 0:128],
                            in_=v_all[c8 * 512:(c8 + 1) * 512, h * 128:(h + 1) * 128].rearrange(
                                "(c p) j -> p c j", p=128)),
                            inc=(kvs, 16))
                    p3.op("sync", lambda g, wb=wb, h=h: g.dma_start(
                        out=wb[:], in_=wq_d[h]), inc=(kvs, 16))
                    ld_h = p3.cnt(kvs)
                    eps_ = f"dEp{h % 2}"
                    w = [("sEO", 8 * (h - 1))] if h >= 2 else []
                    p3.op("gpsimd", lambda g, h=h: g.dma_start(
                        out=x_t[h % 2][:].rearrange("p (qt j) -> p qt j", qt=8),
                        in_=x_d[0:1024, h * 128:(h + 1) * 128].rearrange("(qt p) j -> p qt j", p=128)),
                        waits=w, inc=(eps_, 16))
                    p3.op("gpsimd", lambda g, h=h: g.dma_start(
                        out=f_t[h % 2][:].rearrange("p (qt j) -> p qt j", qt=8),
                        in_=ffn_tm[0:1024, h * 128:(h + 1) * 128].rearrange("(qt p) j -> p qt j", p=128)),
                        inc=(eps_, 16))
                    ep_ld = p3.cnt(eps_)

                    p3.op("scalar", lambda g, kb=kb: g.activation(
                        ksq_sb[:], kb[:], AF.Square), waits=[(kvs, ld_h)], inc=("sKQ", 1))
                    ksq_done = p3.cnt("sKQ")
                    p3.op("vector", lambda g: g.memset(nrm[0:1, 0:2], -1.0e30),
                          waits=[("sNR", p3.cnt("sNR"))], inc=("sNR", 1))
                    red_cnt = {}
                    nm_base = p3.cnt("sNM")

                    def plan_norm(j, h=h):
                        kind, jj = ("k", j) if j < 8 else ("q", j - 8)
                        acc_col = 0 if kind == "k" else 1
                        pn = psN[j % 2]
                        w = [("sKQ", ksq_done)] if kind == "k" else [("sQE", qe_box[0])]
                        if j >= 2:
                            w.append(("sNR", red_cnt[j - 2]))
                        elif h >= 1:
                            w.append(("sBC", prev_bias_ready))
                        srcT = ksq_sb if kind == "k" else qsq_sb
                        incs = []
                        if 1 <= j <= 7 or j == 9:
                            incs.append(("sNM", 1))
                        p3.op("tensor", lambda g, pn=pn, srcT=srcT, jj=jj: g.matmul(
                            pn[0:1, :], ones_col_bf[:], srcT[:, jj * 512:(jj + 1) * 512],
                            start=True, stop=True),
                            waits=w, incs=incs)
                        p3.op("vector", lambda g, pn=pn: g.reduce_max(
                            nrm[0:1, 2:3], pn[0:1, :], axis=AX.X),
                            waits=[("sNM", nm_base + j + 1), ("sNR", p3.cnt("sNR"))], inc=("sNR", 1))
                        red_cnt[j] = p3.cnt("sNR")
                        p3.op("vector", lambda g, acc_col=acc_col: g.tensor_max(
                            nrm[0:1, acc_col:acc_col + 1], nrm[0:1, acc_col:acc_col + 1], nrm[0:1, 2:3]),
                            waits=[("sNR", p3.cnt("sNR"))], inc=("sNR", 1))

                    qe_box = [None]
                    for j in range(8):
                        plan_norm(j)
                    fence3("sNM")
                    w = [(kvs, ld_h)]
                    if h >= 1:
                        w.append(("sQE", 2 * h))
                        w.append(("sEX", exp_cnt[g_pair - 2]))
                    for ic in range(NC):
                        for nb in range(2):
                            first = (nb == 0 and ic == 0)
                            p3.op("tensor", lambda g, nb=nb, ic=ic, wb=wb: g.matmul(
                                ringA[:, nb * 512:(nb + 1) * 512],
                                wb[:, ic * 128:(ic + 1) * 128],
                                xn_own[:, ic * TOK + nb * 512: ic * TOK + (nb + 1) * 512],
                                start=(ic == 0), stop=(ic == NC - 1)),
                                waits=w if first else ())
                    fence3("sQP")
                    p3.op("scalar", lambda g, h=h: g.activation(
                        q_sb[:], ringA[:], AF.Identity, bias=bq_sb[:, h:h + 1], scale=1.0),
                        waits=[("sQP", h + 1)], inc=("sQE", 1))
                    p3.op("scalar", lambda g, h=h: g.activation(
                        qsq_sb[:], ringA[:], AF.Square, bias=bq_sb[:, h:h + 1], scale=1.0),
                        inc=("sQE", 1))
                    qe = p3.cnt("sQE")
                    qe_box[0] = qe
                    plan_norm(8)
                    plan_norm(9)
                    fence3("sNM")
                    p3.op("vector", lambda g: g.tensor_mul(nrm[0:1, 3:4], nrm[0:1, 0:1], nrm[0:1, 1:2]),
                          waits=[("sNR", p3.cnt("sNR"))], inc=("sNR", 1))
                    p3.op("scalar", lambda g: g.activation(
                        nrm[0:1, 4:5], nrm[0:1, 3:4], AF.Sqrt, bias=0.0, scale=1.0 / 128.0),
                        waits=[("sNR", p3.cnt("sNR"))], inc=("sBC", 1))
                    p3.op("vector", lambda g: g.tensor_scalar(
                        nrm[0:1, 5:6], nrm[0:1, 4:5], -1.0, None, ALU.mult),
                        waits=[("sBC", p3.cnt("sBC"))], inc=("sNR", 1))
                    p3.op("tensor", lambda g: g.matmul(
                        psN[0][:, 0:1], ones_row_f[:], nrm[0:1, 5:6], start=True, stop=True),
                        waits=[("sNR", p3.cnt("sNR"))])
                    p3.op("vector", lambda g: g.tensor_copy(bias_sb[:], psN[0][:, 0:1]),
                          waits=[("sSC", p3.cnt("sSC") + 1)], inc=("sBC", 1))
                    bias_ready = p3.cnt("sBC")
                    prev_bias_ready = bias_ready

                    for qh in range(2):
                        slots = [psE[:, 0:129], psE[:, 129:258], psE[:, 258:387], psF[:, 0:129]]
                        pair_base = g_pair

                        def plan_sc(tc, carries_fence_of=None, qh=qh, kb=kb, pair_base=pair_base):
                            ring = ringA if (tc % 4) < 2 else ringB
                            half = tc % 2
                            gp_t = pair_base + tc // 2
                            w = [("sQE", qe)]
                            if gp_t - 2 >= 0:
                                w.append(("sEX", exp_cnt[gp_t - 2]))
                            incs = []
                            if carries_fence_of is not None:
                                incs.append(("sSC", 1))
                            p3.op("tensor", lambda g, tc=tc, qh=qh, kb=kb, ring=ring, half=half: g.matmul(
                                ring[:, half * 512:(half + 1) * 512], kb[:, tc * 128:(tc + 1) * 128],
                                q_sb[:, qh * 512:(qh + 1) * 512], start=True, stop=True),
                                waits=w, incs=incs)
                            if carries_fence_of is not None:
                                fence_cnt[carries_fence_of] = p3.cnt("sSC")

                        plan_sc(0)
                        plan_sc(1)
                        plan_sc(2, carries_fence_of=pair_base)
                        plan_sc(3)
                        for lp in range(16):
                            gp = pair_base + lp
                            ring = ringA if lp % 2 == 0 else ringB
                            p3.op("scalar", lambda g, gp=gp, ring=ring: g.activation(
                                pt_sb[gp % 2][:], ring[:, 0:1024], AF.Exp, bias=bias_sb[:], scale=RSCALE),
                                waits=[("sSC", fence_cnt[gp]), ("sBC", bias_ready)], inc=("sEX", 1))
                            exp_cnt[gp] = p3.cnt("sEX")
                            g_pair += 1
                            for u in range(2):
                                tcp = 2 * lp + u
                                exc = exp_cnt[gp]
                                for qb in range(4):
                                    w2 = ()
                                    incs = []
                                    if qb == 0:
                                        w2 = [("sEX", exc)]
                                        if tcp == 0 and n_ct >= 1:
                                            w2 = [("sEX", exc), ("sCE", n_ct * 2)]
                                        if u == 0 and lp + 1 <= 15:
                                            incs = [("sSC", 1)]
                                    st = (tcp == 0 and qb in (0, 3))
                                    stp = (tcp == NC - 1 and qb in (2, 3))
                                    p3.op("tensor", lambda g, slots=slots, qb=qb, tcp=tcp, vb=vb, gp=gp, u=u, st=st, stp=stp: g.matmul(
                                        slots[qb], pt_sb[gp % 2][:, u * 512 + qb * 128: u * 512 + (qb + 1) * 128],
                                        vb[:, tcp * 129:(tcp + 1) * 129], start=st, stop=stp),
                                        waits=w2, incs=incs)
                                    if incs:
                                        fence_cnt[pair_base + lp + 1] = p3.cnt("sSC")
                            if 2 * lp + 4 < NC:
                                plan_sc(2 * lp + 4)
                                plan_sc(2 * lp + 5)
                        n_ct += 1
                        fence3("sCT")
                        cpar = (n_ct - 1) % 2
                        w = [("sCT", n_ct)]
                        if n_ct >= 3:
                            w.append(("sEO", (n_ct - 2) * 4))
                        p3.op("scalar", lambda g, cpar=cpar: g.activation(
                            ctx_sb[cpar][:, 0:387], psE[:, 0:387], AF.Copy), waits=w, inc=("sCE", 1))
                        p3.op("scalar", lambda g, cpar=cpar: g.activation(
                            ctx_sb[cpar][:, 387:516], psF[:, 0:129], AF.Copy), inc=("sCE", 1))
                        ce = p3.cnt("sCE")
                        for qb in range(4):
                            qt = qh * 4 + qb
                            r0 = qt * 128
                            par2 = n_eo % 2
                            cb = qb * 129
                            c0_ = qt * 128
                            p3.op("vector", lambda g, par2=par2, c0_=c0_, h=h: g.tensor_add(
                                xf_t[par2][:], x_t[h % 2][:, c0_:c0_ + 128], f_t[h % 2][:, c0_:c0_ + 128]),
                                waits=[(eps_, ep_ld), ("sEO", p3.cnt("sEO") - 1)] if n_eo >= 2
                                      else [(eps_, ep_ld)],
                                inc=("sNR", 1))
                            p3.op("vector", lambda g, cpar=cpar, cb=cb: g.reciprocal(
                                rden[:], ctx_sb[cpar][:, cb + 128:cb + 129]),
                                waits=[("sCE", ce), ("sNR", p3.cnt("sNR"))], inc=("sNR", 1))
                            dos = f"dOut{par2}"
                            stt_w = [("sNR", p3.cnt("sNR"))]
                            if n_eo >= 2:
                                stt_w.append((dos, p3.cnt(dos)))
                            p3.op("vector", lambda g, cpar=cpar, cb=cb, par2=par2: g.scalar_tensor_tensor(
                                o_sb[par2][:], ctx_sb[cpar][:, cb:cb + 128], rden[:, 0:1], xf_t[par2][:],
                                ALU.mult, ALU.add),
                                waits=stt_w, inc=("sEO", 1))
                            eo = p3.cnt("sEO")
                            n_eo += 1
                            p3.op("gpsimd", lambda g, par2=par2, r0=r0, h=h: g.dma_start(
                                out=out_d[r0:r0 + 128, h * 128:(h + 1) * 128], in_=o_sb[par2][:]),
                                waits=[("sEO", eo)], inc=(dos, 16))

                p3.op("gpsimd", lambda g: g.nop(), waits=[
                    ("dOut0", p3.cnt("dOut0")), ("dOut1", p3.cnt("dOut1")),
                    ("dEp0", p3.cnt("dEp0")), ("dEp1", p3.cnt("dEp1"))])
                p3.op("sync", lambda g: g.nop(), waits=[("dKVi0", p3.cnt("dKVi0")), ("dKVi1", p3.cnt("dKVi1"))])
                barrier(p3, ringB[0:128, 0:128])
                emit(nc, p3, sems)

    return nc


def _prep_inputs(inp):
    f32 = np.float32
    x = np.asarray(inp["L0_attn_hidden_states"], f32)
    wq = np.asarray(inp["L0_attn_wq"], f32)
    bq = np.asarray(inp["L0_attn_bq"], f32)
    wk = np.asarray(inp["L0_attn_wk"], f32)
    bk = np.asarray(inp["L0_attn_bk"], f32)
    wv = np.asarray(inp["L0_attn_wv"], f32)
    bv = np.asarray(inp["L0_attn_bv"], f32)
    gw = np.asarray(inp["L0_ffn_gate_w"])
    uw = np.asarray(inp["L0_ffn_up_w"])
    dw = np.asarray(inp["L0_ffn_down_w"])

    def chunk_oc(wT):
        a = wT.reshape(NC, 128, NC, 128).transpose(2, 1, 0, 3).reshape(NC, 128, D)
        return np.ascontiguousarray(a).astype(bfloat16)

    def chunk_f(wT):
        a = wT.reshape(NC, 128, NF, 128).transpose(2, 1, 0, 3).reshape(NF, 128, D)
        return np.ascontiguousarray(a).astype(bfloat16)

    com = {
        "wq": chunk_oc(wq.T),
        "wk": chunk_oc(wk.T),
        "wv": np.ascontiguousarray(wv.T).astype(bfloat16),
        "bq": np.ascontiguousarray(bq.reshape(NH, 128).T).astype(f32),
        "bk": np.ascontiguousarray(bk.reshape(NC, 128).T).astype(f32),
        "bv": bv.astype(bfloat16).reshape(1, D),
        "gw": chunk_f(gw.T.astype(f32)),
        "uw": chunk_f(uw.T.astype(f32)),
        "dw": np.ascontiguousarray(
            dw.T.astype(f32).reshape(NF, 128, 8, 512).transpose(2, 0, 1, 3)).astype(bfloat16),
        "gs": np.asarray(inp["L0_ffn_gate_s"], f32).reshape(1, 1),
        "us": np.asarray(inp["L0_ffn_up_s"], f32).reshape(1, 1),
        "ds": np.asarray(inp["L0_ffn_down_s"], f32).reshape(1, 1),
    }
    in_maps = []
    for c in range(8):
        b, qr = c // 4, c % 4
        m = dict(com)
        m["x"] = np.ascontiguousarray(np.roll(x[b], -qr * TOK, axis=0))
        in_maps.append(m)
    return in_maps


def kernel(**inp):
    in_maps = _prep_inputs(inp)
    nc = build(debug=False)
    res = run_bass_kernel_spmd(nc, in_maps, core_ids=list(range(8)))
    out = np.empty((2, S, D), dtype=np.float32)
    for c in range(8):
        b, qr = c // 4, c % 4
        out[b, qr * TOK:(qr + 1) * TOK, :] = res.results[c]["out"]
    return out


# version 06 (final)
# speedup vs baseline: 1.1978x; 1.1978x over previous
import numpy as np
import ml_dtypes
from contextlib import ExitStack

import concourse.bass as bass
import concourse.mybir as mybir
from concourse.bass_utils import run_bass_kernel_spmd

F32 = mybir.dt.float32
BF16 = mybir.dt.bfloat16
AF = mybir.ActivationFunctionType
ALU = mybir.AluOpType
AX = mybir.AxisListType
bfloat16 = ml_dtypes.bfloat16

D = 4096
S = 4096
TOK = 1024
NH = 32
NI = 11008
NF = NI // 128
NC = D // 128
NQR = 4
NT = 8
RSCALE = 0.08838834764831845

ENGS = ("sync", "tensor", "scalar", "vector", "gpsimd")


class Prog:
    def __init__(self, counts):
        self.ops = {e: [] for e in ENGS}
        self.n = counts

    def cnt(self, sem):
        return self.n.get(sem, 0)

    def op(self, eng, fn, waits=(), inc=None, incs=()):
        ivs = list(incs)
        if inc is not None:
            ivs.append(inc)
        for sem, k in ivs:
            self.n[sem] = self.n.get(sem, 0) + k
        self.ops[eng].append((tuple(waits), fn, tuple(ivs)))


def emit(nc, prog, sems):
    with nc.Block() as blk:
        def mk(name):
            ops = prog.ops[name]
            def run(eng):
                for waits, fn, ivs in ops:
                    for sem, val in waits:
                        eng.wait_ge(sems[sem], val)
                    inst = fn(eng)
                    for sem, k in ivs:
                        inst = inst.then_inc(sems[sem], k)
            return run
        blk.sync(mk("sync"))
        blk.tensor(mk("tensor"))
        blk.scalar(mk("scalar"))
        blk.vector(mk("vector"))
        blk.gpsimd(mk("gpsimd"))


def build(debug=False, phases=(1, 2, 3)):
    nc = bass.Bass("TRN2", target_bir_lowering=False)

    x_d = nc.dram_tensor("x", [S, D], F32, kind="ExternalInput")
    wq_d = nc.dram_tensor("wq", [NH, 128, D], BF16, kind="ExternalInput")
    wk_d = nc.dram_tensor("wk", [NC, 128, D], BF16, kind="ExternalInput")
    wv_d = nc.dram_tensor("wv", [D, D], BF16, kind="ExternalInput")
    bq_d = nc.dram_tensor("bq", [128, NH], F32, kind="ExternalInput")
    bk_d = nc.dram_tensor("bk", [128, NC], F32, kind="ExternalInput")
    bv_d = nc.dram_tensor("bv", [1, D], BF16, kind="ExternalInput")
    gw_d = nc.dram_tensor("gw", [NF, 128, D], BF16, kind="ExternalInput")
    uw_d = nc.dram_tensor("uw", [NF, 128, D], BF16, kind="ExternalInput")
    dw_d = nc.dram_tensor("dw", [8, NF, 128, 512], BF16, kind="ExternalInput")
    gs_d = nc.dram_tensor("gs", [1, 1], F32, kind="ExternalInput")
    us_d = nc.dram_tensor("us", [1, 1], F32, kind="ExternalInput")
    ds_d = nc.dram_tensor("ds", [1, 1], F32, kind="ExternalInput")
    out_d = nc.dram_tensor("out", [TOK, D], F32, kind="ExternalOutput")

    dbg_kind = "ExternalOutput" if debug else "Internal"
    kt_all = nc.dram_tensor("kt_all", [D, S], BF16, kind=dbg_kind)
    v_all = nc.dram_tensor("v_all", [S, D], BF16, kind=dbg_kind)
    ffn_tm = nc.dram_tensor("ffn_tm", [TOK, D], F32, kind=dbg_kind)
    if debug:
        xn_dbg = nc.dram_tensor("xn_dbg", [128, NC * TOK], BF16, kind="ExternalOutput")

    ident_d = nc.inline_tensor(np.eye(128, dtype=bfloat16), name="ident")

    counts = {}
    sem_names = [
        "ph",
        "dX0", "dX1", "dWk0", "dWk1", "dWv0", "dWv1", "dWv2", "dWv3", "dWg0", "dWg1", "dWu0", "dWu1", "dWd0", "dWd1", "dWd2", "dWd3", "dKVi0", "dKVi1", "dKo0", "dKo1", "dVo0", "dVo1", "dEp0", "dEp1", "dOut0", "dOut1", "dC", "dFO",
        "sLNd", "sLNa", "sXNT", "sTP", "sXNC",
        "sKP", "sKE", "sVP", "sVS", "sVE",
        "sGP", "sUP", "sSI", "sGU", "sDP", "sDS", "sDE",
        "sQP", "sQE", "sKQ", "sNM", "sNR", "sBC",
        "sSC", "sEX", "sCT", "sEO", "sCE",
        "sP0",
    ]

    with ExitStack() as outer:
        sems = {s: outer.enter_context(nc.semaphore(s)) for s in sem_names}

        id_sb = outer.enter_context(nc.sbuf_tensor("id_sb", [128, 128], BF16))
        ones_row_bf = outer.enter_context(nc.sbuf_tensor("ones_row_bf", [1, 128], BF16))
        ones_row_f = outer.enter_context(nc.sbuf_tensor("ones_row_f", [1, 128], F32))
        ones_col_bf = outer.enter_context(nc.sbuf_tensor("ones_col_bf", [128, 1], BF16))
        bq_sb = outer.enter_context(nc.sbuf_tensor("bq_sb", [128, NH], F32))
        bk_sb = outer.enter_context(nc.sbuf_tensor("bk_sb", [128, NC], F32))
        bv_sb = outer.enter_context(nc.sbuf_tensor("bv_sb", [1, D], BF16))
        gs_sb = outer.enter_context(nc.sbuf_tensor("gs_sb", [1, 1], F32))
        us_sb = outer.enter_context(nc.sbuf_tensor("us_sb", [1, 1], F32))
        ds_sb = outer.enter_context(nc.sbuf_tensor("ds_sb", [1, 1], F32))
        usds1 = outer.enter_context(nc.sbuf_tensor("usds1", [1, 1], F32))
        gs_bc = outer.enter_context(nc.sbuf_tensor("gs_bc", [128, 1], F32))
        usds_bc = outer.enter_context(nc.sbuf_tensor("usds_bc", [128, 1], F32))
        eps_sb = outer.enter_context(nc.sbuf_tensor("eps_sb", [128, 1], F32))
        scr1 = outer.enter_context(nc.sbuf_tensor("scr1", [1, 2], F32))
        xn_own = outer.enter_context(nc.sbuf_tensor("xn_own", [128, NC * TOK], BF16))

        def barrier(p, fence_ap):
            base = p.cnt("ph")
            p.op("sync", lambda g: g.nop(), inc=("ph", 1))
            p.op("gpsimd", lambda g: g.nop(), inc=("ph", 1))
            p.op("tensor", lambda g, fp=fence_ap: g.matmul(
                fp, ones_row_bf[:], ones_row_bf[:], start=True, stop=True))
            p.op("tensor", lambda g, fp=fence_ap: g.matmul(
                fp, ones_row_bf[:], ones_row_bf[:], start=True, stop=True), inc=("ph", 1))
            p.op("vector", lambda g: g.memset(scr1[0:1, 0:1], 0.0), inc=("ph", 1))
            p.op("scalar", lambda g: g.activation(scr1[0:1, 1:2], gs_sb[0:1, 0:1], AF.Square),
                 waits=[("dC", p.cnt("dC"))], inc=("ph", 1))
            tgt = base + 5
            for e2 in ENGS:
                p.op(e2, lambda g: g.nop(), waits=[("ph", tgt)])

        p0 = Prog(counts)
        with ExitStack() as pctx:
            ps0 = pctx.enter_context(nc.psum_tensor("ps0", [128, 2], F32))
            ps0b = pctx.enter_context(nc.psum_tensor("ps0b", [128, 128], F32))
            for dst, src in [(id_sb, ident_d), (bq_sb, bq_d), (bk_sb, bk_d), (bv_sb, bv_d),
                             (gs_sb, gs_d), (us_sb, us_d), (ds_sb, ds_d)]:
                p0.op("gpsimd", lambda g, dst=dst, src=src: g.dma_start(out=dst[:], in_=src[:, :]),
                      inc=("dC", 16))
            dc_all = p0.cnt("dC")
            p0.op("vector", lambda g: g.memset(ones_row_bf[:], 1.0), inc=("sP0", 1))
            p0.op("vector", lambda g: g.memset(ones_row_f[:], 1.0), inc=("sP0", 1))
            p0.op("vector", lambda g: g.memset(ones_col_bf[:], 1.0), inc=("sP0", 1))
            p0.op("vector", lambda g: g.memset(eps_sb[:], 1e-5), inc=("sP0", 1))
            p0.op("vector", lambda g: g.tensor_mul(usds1[:], us_sb[:], ds_sb[:]),
                  waits=[("dC", dc_all), ("sP0", 4)], inc=("sP0", 1))
            p0.op("tensor", lambda g: g.matmul(ps0[:, 0:1], ones_row_f[:], gs_sb[:], start=True, stop=True),
                  waits=[("sP0", 5), ("dC", dc_all)])
            p0.op("tensor", lambda g: g.matmul(ps0[:, 1:2], ones_row_f[:], usds1[:], start=True, stop=True))
            p0.op("tensor", lambda g: g.matmul(ps0b[:, 0:128], ones_row_bf[:], ones_row_bf[:],
                                               start=True, stop=True), inc=("sP0", 1))
            p0.op("vector", lambda g: g.tensor_copy(gs_bc[:], ps0[:, 0:1]), waits=[("sP0", 6)])
            p0.op("vector", lambda g: g.tensor_copy(usds_bc[:], ps0[:, 1:2]), inc=("sP0", 1))
            p0.op("gpsimd", lambda g: g.nop(), waits=[("dC", dc_all)])
            barrier(p0, ps0b[:, 0:128])
            emit(nc, p0, sems)

        if 1 in phases:
            p1 = Prog(counts)
            with ExitStack() as pctx:
                e = pctx.enter_context
                x_sb = [e(nc.sbuf_tensor(f"x_sb{i}", [128, D], F32)) for i in range(2)]
                xn_tmp = e(nc.sbuf_tensor("xn_tmp", [128, NC * TOK], BF16))
                xn_tm = e(nc.sbuf_tensor("xn_tm", [128, D], BF16))
                stat = e(nc.sbuf_tensor("stat", [128, 8], F32))
                wk_sb = [e(nc.sbuf_tensor(f"wk_sb{i}", [128, D], BF16)) for i in range(2)]
                wv_sb = [e(nc.sbuf_tensor(f"wv_sb{i}", [128, 512], BF16)) for i in range(4)]
                k_ev = e(nc.sbuf_tensor("k_ev", [128, TOK], BF16))
                v_ev = [e(nc.sbuf_tensor(f"v_ev{i}", [128, 4 * 512], BF16)) for i in range(2)]
                psA = e(nc.psum_tensor("psA", [128, 1024], F32))
                psB = e(nc.psum_tensor("psB", [128, 1024], F32))
                tpp = [e(nc.psum_tensor(f"tp{i}", [128, 1024], BF16)) for i in range(2)]
                tpF = e(nc.psum_tensor("tpF", [128, 128], F32))

                def fence1(sem):
                    p1.op("tensor", lambda g: g.matmul(
                        tpF[:, 0:128], ones_row_bf[:], ones_row_bf[:], start=True, stop=True),
                        inc=(sem, 1))

                n_xtile = 0
                n_group_v = 0
                kst = {}
                vst = {}
                for qr in range(NQR):
                    xn_fm = xn_own if qr == 0 else xn_tmp
                    for ti in range(NT):
                        xb = x_sb[n_xtile % 2]
                        r0 = qr * TOK + ti * 128
                        dxs = f"dX{n_xtile % 2}"
                        w = [("sXNT", n_xtile - 1)] if n_xtile >= 2 else []
                        p1.op("gpsimd", lambda g, xb=xb, r0=r0: g.dma_start(
                            out=xb[:], in_=x_d[r0:r0 + 128, :]), waits=w, inc=(dxs, 16))
                        ld = p1.cnt(dxs)
                        p1.op("vector", lambda g, xb=xb: g.reduce_sum(
                            stat[:, 0:1], xb[:], axis=AX.X),
                            waits=[(dxs, ld), ("sLNd", p1.cnt("sLNd"))], inc=("sLNd", 1))
                        p1.op("scalar", lambda g, xb=xb: g.activation(
                            xn_tm[:], xb[:], AF.Square, accum_out=stat[:, 1:2]),
                            waits=[(dxs, ld), ("sTP", 8 * n_xtile)], inc=("sLNa", 1))
                        a0 = p1.cnt("sLNa")
                        d0 = p1.cnt("sLNd")
                        p1.op("vector", lambda g: g.tensor_scalar(
                            stat[:, 2:3], stat[:, 0:1], 1.0 / D, None, ALU.mult),
                            waits=[("sLNd", d0)], inc=("sLNd", 1))
                        p1.op("vector", lambda g: g.tensor_scalar(
                            stat[:, 6:7], stat[:, 1:2], 1.0 / D, None, ALU.mult),
                            waits=[("sLNa", a0), ("sLNd", p1.cnt("sLNd"))], inc=("sLNd", 1))
                        p1.op("vector", lambda g: g.scalar_tensor_tensor(
                            stat[:, 3:4], stat[:, 2:3], -1.0, stat[:, 2:3], ALU.mult, ALU.mult),
                            waits=[("sLNd", p1.cnt("sLNd"))], inc=("sLNd", 1))
                        p1.op("vector", lambda g: g.tensor_add(
                            stat[:, 3:4], stat[:, 3:4], stat[:, 6:7]),
                            waits=[("sLNd", p1.cnt("sLNd"))], inc=("sLNd", 1))
                        d1 = p1.cnt("sLNd")
                        p1.op("scalar", lambda g: g.activation(
                            stat[:, 7:8], stat[:, 3:4], AF.Sqrt, bias=eps_sb[:], scale=1.0),
                            waits=[("sLNd", d1)], inc=("sLNa", 1))
                        a1 = p1.cnt("sLNa")
                        p1.op("vector", lambda g: g.reciprocal(
                            stat[:, 4:5], stat[:, 7:8]),
                            waits=[("sLNa", a1), ("sLNd", p1.cnt("sLNd"))], inc=("sLNd", 1))
                        p1.op("vector", lambda g: g.tensor_scalar(
                            stat[:, 5:6], stat[:, 2:3], stat[:, 4:5], -1.0, ALU.mult, ALU.mult),
                            waits=[("sLNd", p1.cnt("sLNd"))], inc=("sLNd", 1))
                        p1.op("vector", lambda g, xb=xb: g.tensor_scalar(
                            xn_tm[:], xb[:], stat[:, 4:5], stat[:, 5:6], ALU.mult, ALU.add),
                            waits=[("sLNd", p1.cnt("sLNd"))], inc=("sXNT", 1))
                        xnt = p1.cnt("sXNT")
                        for grp in range(8):
                            tp = tpp[grp % 2]
                            gidx = n_xtile * 8 + grp
                            w = [("sXNT", xnt)]
                            if gidx >= 2:
                                w.append(("sXNC", (gidx - 1) * 4))
                            for j4 in range(4):
                                ic = grp * 4 + j4
                                p1.op("tensor", lambda g, tp=tp, j4=j4, ic=ic: g.transpose(
                                    tp[:, j4 * 128:(j4 + 1) * 128], xn_tm[:, ic * 128:(ic + 1) * 128], id_sb[:]),
                                    waits=w if j4 == 0 else (),
                                    inc=("sTP", 1) if (j4 == 0 and grp >= 1) else None)
                            tpc_fence = p1.cnt("sTP") + 1
                            for j4 in range(4):
                                ic = grp * 4 + j4
                                col = ic * TOK + ti * 128
                                p1.op("vector", lambda g, tp=tp, j4=j4, xn_fm=xn_fm, col=col: g.tensor_copy(
                                    xn_fm[:, col:col + 128], tp[:, j4 * 128:(j4 + 1) * 128]),
                                    waits=[("sTP", tpc_fence)] if j4 == 0 else (),
                                    inc=("sXNC", 1))
                        fence1("sTP")
                        n_xtile += 1
                    xnc_full = p1.cnt("sXNC")

                    for oc in range(NC):
                        wkb = wk_sb[oc % 2]
                        gi = qr * NC + oc
                        wks = f"dWk{oc % 2}"
                        w = [("sKP", gi - 1)] if gi >= 2 else []
                        p1.op("sync", lambda g, wkb=wkb, oc=oc: g.dma_start(
                            out=wkb[:], in_=wk_d[oc]), waits=w, inc=(wks, 16))
                        ldw = p1.cnt(wks)
                        acc = psA if oc % 2 == 0 else psB
                        w = [(wks, ldw), ("sXNC", xnc_full)]
                        if gi >= 2:
                            w.append(("sKE", gi - 1))
                        if qr >= 1 and oc < 2:
                            w.append(("sVE", 64 * qr))
                        for ic in range(NC):
                            for nb in range(2):
                                first = (nb == 0 and ic == 0)
                                p1.op("tensor", lambda g, acc=acc, nb=nb, ic=ic, wkb=wkb, xn_fm=xn_fm: g.matmul(
                                    acc[:, nb * 512:(nb + 1) * 512],
                                    wkb[:, ic * 128:(ic + 1) * 128],
                                    xn_fm[:, ic * TOK + nb * 512: ic * TOK + (nb + 1) * 512],
                                    start=(ic == 0), stop=(ic == NC - 1)),
                                    waits=w if first else (),
                                    inc=("sKP", 1) if (first and oc >= 1) else None)
                        if oc == NC - 1:
                            fence1("sKP")
                        kb = k_ev
                        kos = "dKo0"
                        w = [("sKP", gi + 1)]
                        if gi >= 1:
                            w.append((kos, kst[gi - 1]))
                        p1.op("scalar", lambda g, kb=kb, acc=acc, oc=oc: g.activation(
                            kb[:], acc[:], AF.Identity, bias=bk_sb[:, oc:oc + 1], scale=1.0),
                            waits=w, inc=("sKE", 1))
                        ke = p1.cnt("sKE")
                        p1.op("gpsimd", lambda g, kb=kb, oc=oc, qr=qr: g.dma_start(
                            out=kt_all[oc * 128:(oc + 1) * 128, qr * TOK:(qr + 1) * TOK], in_=kb[:]),
                            waits=[("sKE", ke)], inc=(kos, 16))
                        kst[gi] = p1.cnt(kos)

                    for ttg in range(2):
                        for db in range(8):
                            gi = n_group_v
                            accs = [psA[:, 0:512], psA[:, 512:1024], psB[:, 0:512], psB[:, 512:1024]]
                            for ic in range(NC):
                                wvb = wv_sb[ic % 4]
                                gsl = gi * NC + ic
                                wvs = f"dWv{ic % 4}"
                                w = [("sVS", gsl - 3)] if gsl >= 4 else []
                                p1.op("sync", lambda g, wvb=wvb, ic=ic, db=db: g.dma_start(
                                    out=wvb[:], in_=wv_d[ic * 128:(ic + 1) * 128, db * 512:(db + 1) * 512]),
                                    waits=w, inc=(wvs, 16))
                                ldv = p1.cnt(wvs)
                                for tt in range(4):
                                    t_idx = ttg * 4 + tt
                                    w2 = [(wvs, ldv)] if tt == 0 else []
                                    if ic == 0:
                                        if gi % 16 == 0:
                                            if tt == 0:
                                                w2.append(("sKE", p1.cnt("sKE")))
                                        else:
                                            w2.append(("sVE", 4 * (gi - 1) + tt + 1))
                                    p1.op("tensor", lambda g, accs=accs, tt=tt, ic=ic, t_idx=t_idx, wvb=wvb, xn_fm=xn_fm: g.matmul(
                                        accs[tt], xn_fm[:, ic * TOK + t_idx * 128: ic * TOK + t_idx * 128 + 128],
                                        wvb[:], start=(ic == 0), stop=False),
                                        waits=w2,
                                        inc=("sVS", 1) if (tt == 0 and ic >= 1) else None)
                            for tt in range(4):
                                p1.op("tensor", lambda g, accs=accs, tt=tt, db=db: g.matmul(
                                    accs[tt], ones_row_bf[:], bv_sb[:, db * 512:(db + 1) * 512],
                                    start=False, stop=True),
                                    inc=("sVS", 1) if tt == 0 else None)
                            fence1("sVP")
                            vb = v_ev[gi % 2]
                            vos = f"dVo{gi % 2}"
                            w = [("sVP", gi + 1)]
                            if gi >= 2:
                                w.append((vos, vst[gi - 2]))
                            for tt in range(4):
                                p1.op("scalar", lambda g, vb=vb, accs=accs, tt=tt: g.activation(
                                    vb[:, tt * 512:(tt + 1) * 512], accs[tt], AF.Copy),
                                    waits=w if tt == 0 else (), inc=("sVE", 1))
                            ve = p1.cnt("sVE")
                            for tt in range(4):
                                t_idx = ttg * 4 + tt
                                r0 = qr * TOK + t_idx * 128
                                p1.op("gpsimd", lambda g, vb=vb, tt=tt, r0=r0, db=db: g.dma_start(
                                    out=v_all[r0:r0 + 128, db * 512:(db + 1) * 512],
                                    in_=vb[:, tt * 512:(tt + 1) * 512]),
                                    waits=[("sVE", ve)] if tt == 0 else (), inc=(vos, 16))
                            vst[gi] = p1.cnt(vos)
                            n_group_v += 1

                if debug:
                    p1.op("gpsimd", lambda g: g.dma_start(out=xn_dbg[:, :], in_=xn_own[:, :]),
                          waits=[("sXNC", 256)], inc=("dFO", 16))
                p1.op("gpsimd", lambda g: g.nop(), waits=[
                    ("dX0", p1.cnt("dX0")), ("dX1", p1.cnt("dX1")),
                    ("dKo0", p1.cnt("dKo0")), ("dKo1", p1.cnt("dKo1")),
                    ("dVo0", p1.cnt("dVo0")), ("dVo1", p1.cnt("dVo1")),
                    ("dFO", p1.cnt("dFO"))])
                p1.op("sync", lambda g: g.nop(), waits=[
                    ("dWk0", p1.cnt("dWk0")), ("dWk1", p1.cnt("dWk1"))] + [
                    (f"dWv{i}", p1.cnt(f"dWv{i}")) for i in range(4)])
                barrier(p1, tpF[:, 0:128])
                emit(nc, p1, sems)

        if 2 in phases:
            p2 = Prog(counts)
            with ExitStack() as pctx:
                e = pctx.enter_context
                gw_sb = [e(nc.sbuf_tensor(f"gw_sb{i}", [128, D], BF16)) for i in range(2)]
                uw_sb = [e(nc.sbuf_tensor(f"uw_sb{i}", [128, D], BF16)) for i in range(2)]
                dw_sb = [e(nc.sbuf_tensor(f"dw_sb{i}", [128, 512], BF16)) for i in range(4)]
                silu_sb = [e(nc.sbuf_tensor(f"silu_sb{i}", [128, 512], BF16)) for i in range(2)]
                gu_sb = e(nc.sbuf_tensor("gu_sb", [128, NF * 512], BF16))
                f_ev = e(nc.sbuf_tensor("f_ev", [128, 4 * 512], F32))
                psg = [e(nc.psum_tensor(f"psg{i}", [128, 512], F32)) for i in range(2)]
                psu = [e(nc.psum_tensor(f"psu{i}", [128, 512], F32)) for i in range(2)]
                psd = [e(nc.psum_tensor(f"psd{i}", [128, 512], F32)) for i in range(4)]

                fo_cnt = {}
                for th in range(2):
                    t0 = th * 512
                    de_prev = p2.cnt("sDE")
                    for f in range(NF):
                        gi = th * NF + f
                        gwb, uwb = gw_sb[f % 2], uw_sb[f % 2]
                        wgs, wus = f"dWg{f % 2}", f"dWu{f % 2}"
                        w = [("sUP", gi - 1)] if gi >= 2 else []
                        p2.op("sync", lambda g, gwb=gwb, f=f: g.dma_start(out=gwb[:], in_=gw_d[f]),
                              waits=w, inc=(wgs, 16))
                        p2.op("sync", lambda g, uwb=uwb, f=f: g.dma_start(out=uwb[:], in_=uw_d[f]),
                              inc=(wus, 16))
                        ldg, ldu = p2.cnt(wgs), p2.cnt(wus)
                        pg, pu = psg[f % 2], psu[f % 2]
                        w = [(wgs, ldg)]
                        if gi >= 2:
                            w.append(("sSI", gi - 1))
                        if th >= 1 and f < 2:
                            w.append(("sDE", de_prev))
                        for ic in range(NC):
                            p2.op("tensor", lambda g, pg=pg, gwb=gwb, ic=ic, t0=t0: g.matmul(
                                pg[:], gwb[:, ic * 128:(ic + 1) * 128],
                                xn_own[:, ic * TOK + t0: ic * TOK + t0 + 512],
                                start=(ic == 0), stop=(ic == NC - 1)),
                                waits=w if ic == 0 else (),
                                inc=("sUP", 1) if (ic == 0 and f >= 1) else None)
                        w = [(wus, ldu)]
                        if gi >= 2:
                            w.append(("sGU", gi - 1))
                        if th >= 1 and f < 2:
                            w.append(("sDE", de_prev))
                        for ic in range(NC):
                            p2.op("tensor", lambda g, pu=pu, uwb=uwb, ic=ic, t0=t0: g.matmul(
                                pu[:], uwb[:, ic * 128:(ic + 1) * 128],
                                xn_own[:, ic * TOK + t0: ic * TOK + t0 + 512],
                                start=(ic == 0), stop=(ic == NC - 1)),
                                waits=w if ic == 0 else (),
                                inc=("sGP", 1) if ic == 0 else None)
                        sl = silu_sb[f % 2]
                        w = [("sGP", gi + 1)]
                        if gi >= 2:
                            w.append(("sGU", gi - 1))
                        p2.op("scalar", lambda g, sl=sl, pg=pg: g.activation(
                            sl[:], pg[:], AF.Silu, bias=0.0, scale=gs_bc[:, 0:1]),
                            waits=w, inc=("sSI", 1))
                        si = p2.cnt("sSI")
                        p2.op("vector", lambda g, sl=sl, pu=pu, f=f: g.tensor_tensor(
                            gu_sb[:, f * 512:(f + 1) * 512], sl[:], pu[:], ALU.mult),
                            waits=[("sSI", si), ("sUP", gi + 1)], inc=("sGU", 1))
                    p2.op("tensor", lambda g: g.matmul(
                        psd[0][0:128, 0:128], ones_row_bf[:], ones_row_bf[:], start=True, stop=True),
                        inc=("sUP", 1))
                    gu_all = (th + 1) * NF

                    for db in range(8):
                        gi = th * 8 + db
                        accs = [psd[0][:], psd[1][:], psd[2][:], psd[3][:]] if db % 2 == 0 else \
                               [psg[0][:], psg[1][:], psu[0][:], psu[1][:]]
                        for f in range(NF):
                            gf = gi * NF + f
                            dwb = dw_sb[gf % 4]
                            wds = f"dWd{gf % 4}"
                            w = [("sDS", gf - 3)] if gf >= 4 else []
                            p2.op("sync", lambda g, dwb=dwb, db=db, f=f: g.dma_start(
                                out=dwb[:], in_=dw_d[db, f]), waits=w, inc=(wds, 16))
                            ldd = p2.cnt(wds)
                            for tb in range(4):
                                w2 = [(wds, ldd)]
                                if f == 0 and tb == 0:
                                    w2.append(("sGU", gu_all))
                                    if gi >= 2:
                                        w2.append(("sDE", (gi - 2) * 4 + 4))
                                incs = []
                                if tb == 0 and gf >= 1:
                                    incs.append(("sDS", 1))
                                if tb == 1 and f == 0 and db >= 1:
                                    incs.append(("sDP", 1))
                                p2.op("tensor", lambda g, accs=accs, tb=tb, f=f, dwb=dwb: g.matmul(
                                    accs[tb], gu_sb[:, f * 512 + tb * 128: f * 512 + tb * 128 + 128],
                                    dwb[:], start=(f == 0), stop=(f == NF - 1)),
                                    waits=w2 if tb == 0 else (), incs=incs)
                        fb = f_ev
                        w = [("sDP", gi + 1)]
                        if gi >= 1:
                            w.append(("dFO", fo_cnt[gi - 1]))
                        for tb in range(4):
                            p2.op("vector", lambda g, fb=fb, accs=accs, tb=tb: g.tensor_scalar(
                                fb[:, tb * 512:(tb + 1) * 512], accs[tb], usds_bc[:, 0:1], None, ALU.mult),
                                waits=w if tb == 0 else (), inc=("sDE", 1))
                            de = p2.cnt("sDE")
                            r0 = t0 + tb * 128
                            p2.op("gpsimd", lambda g, fb=fb, tb=tb, r0=r0, db=db: g.dma_start(
                                out=ffn_tm[r0:r0 + 128, db * 512:(db + 1) * 512],
                                in_=fb[:, tb * 512:(tb + 1) * 512]),
                                waits=[("sDE", de)], inc=("dFO", 16))
                        fo_cnt[gi] = p2.cnt("dFO")
                    p2.op("tensor", lambda g: g.matmul(
                        psd[0][0:128, 0:128], ones_row_bf[:], ones_row_bf[:], start=True, stop=True),
                        waits=[("sDE", (th * 8 + 7) * 4)], inc=("sDP", 1))
                p2.op("tensor", lambda g: g.matmul(
                    psd[0][0:128, 0:128], ones_row_bf[:], ones_row_bf[:], start=True, stop=True),
                    waits=[("sDE", p2.cnt("sDE"))], inc=("sDS", 1))
                p2.op("tensor", lambda g: g.nop(), waits=[("sDE", p2.cnt("sDE"))])
                p2.op("gpsimd", lambda g: g.nop(), waits=[("dFO", p2.cnt("dFO"))])
                p2.op("sync", lambda g: g.nop(), waits=[
                    ("dWg0", p2.cnt("dWg0")), ("dWg1", p2.cnt("dWg1")),
                    ("dWu0", p2.cnt("dWu0")), ("dWu1", p2.cnt("dWu1"))] + [
                    (f"dWd{i}", p2.cnt(f"dWd{i}")) for i in range(4)])
                barrier(p2, psd[0][0:128, 0:128])
                emit(nc, p2, sems)

        if 3 in phases:
            p3 = Prog(counts)
            with ExitStack() as pctx:
                e = pctx.enter_context
                khs = [e(nc.sbuf_tensor(f"khs{i}", [128, S], BF16)) for i in range(2)]
                vhs = [e(nc.sbuf_tensor(f"vhs{i}", [128, 32 * 129], BF16)) for i in range(2)]
                wqs = [e(nc.sbuf_tensor(f"wqs{i}", [128, D], BF16)) for i in range(2)]
                q_sb = e(nc.sbuf_tensor("q_sb", [128, TOK], BF16))
                qsq_sb = e(nc.sbuf_tensor("qsq_sb", [128, TOK], BF16))
                ksq_sb = e(nc.sbuf_tensor("ksq_sb", [128, S], BF16))
                pt_sb = [e(nc.sbuf_tensor(f"pt_sb{i}", [128, 1024], BF16)) for i in range(3)]
                nrm = e(nc.sbuf_tensor("nrm", [1, 8], F32))
                bias_sb = e(nc.sbuf_tensor("bias_sb", [128, 1], F32))
                rden = e(nc.sbuf_tensor("rden", [128, 1], F32))
                x_t = [e(nc.sbuf_tensor(f"x_t{i}", [128, 8 * 128], F32)) for i in range(2)]
                f_t = [e(nc.sbuf_tensor(f"f_t{i}", [128, 8 * 128], F32)) for i in range(2)]
                xf_t = [e(nc.sbuf_tensor(f"xf_t{i}", [128, 128], F32)) for i in range(2)]
                o_sb = [e(nc.sbuf_tensor(f"o_sb{i}", [128, 128], F32)) for i in range(2)]
                ctx_sb = [e(nc.sbuf_tensor(f"ctx_sb{i}", [128, 516], F32)) for i in range(2)]
                ringA = e(nc.psum_tensor("ringA", [128, 1024], F32))
                ringB = e(nc.psum_tensor("ringB", [128, 1024], F32))
                ringC = e(nc.psum_tensor("ringC", [128, 1024], F32))
                rings = [ringA, ringB, ringC]
                psE = e(nc.psum_tensor("psE", [128, 512], F32))
                psF = e(nc.psum_tensor("psF", [128, 512], F32))
                psN = [psE, psF]

                def fence3(sem, waits=()):
                    p3.op("tensor", lambda g: g.matmul(
                        ringB[0:128, 0:128], ones_row_bf[:], ones_row_bf[:], start=True, stop=True),
                        waits=waits, inc=(sem, 1))

                for i in range(2):
                    p3.op("vector", lambda g, i=i: g.memset(
                        vhs[i][:, :].rearrange("p (c k) -> p c k", k=129)[:, :, 128:129], 1.0),
                        inc=("sP0", 1))
                ones_done = p3.cnt("sP0")

                n_ct = 0
                n_eo = 0
                prev_bias_ready = 0
                g_pair = 0
                exp_cnt = {}
                fence_cnt = {}
                for h in range(NH):
                    par = h % 2
                    kb, vb, wb = khs[par], vhs[par], wqs[par]
                    kvs = f"dKVi{h % 2}"
                    w = [("sCT", (h - 2) * 2 + 2)] if h >= 2 else [("sP0", ones_done)]
                    p3.op("sync", lambda g, kb=kb, h=h: g.dma_start(
                        out=kb[:], in_=kt_all[h * 128:(h + 1) * 128, :]), waits=w, inc=(kvs, 16))
                    for c8 in range(8):
                        p3.op("sync", lambda g, vb=vb, h=h, c8=c8: g.dma_start(
                            out=vb[:, c8 * 4 * 129:(c8 + 1) * 4 * 129].rearrange(
                                "p (c k) -> p c k", k=129)[:, :, 0:128],
                            in_=v_all[c8 * 512:(c8 + 1) * 512, h * 128:(h + 1) * 128].rearrange(
                                "(c p) j -> p c j", p=128)),
                            inc=(kvs, 16))
                    p3.op("sync", lambda g, wb=wb, h=h: g.dma_start(
                        out=wb[:], in_=wq_d[h]), inc=(kvs, 16))
                    ld_h = p3.cnt(kvs)
                    eps_ = f"dEp{h % 2}"
                    w = [("sEO", 8 * (h - 1))] if h >= 2 else []
                    p3.op("gpsimd", lambda g, h=h: g.dma_start(
                        out=x_t[h % 2][:].rearrange("p (qt j) -> p qt j", qt=8),
                        in_=x_d[0:1024, h * 128:(h + 1) * 128].rearrange("(qt p) j -> p qt j", p=128)),
                        waits=w, inc=(eps_, 16))
                    p3.op("gpsimd", lambda g, h=h: g.dma_start(
                        out=f_t[h % 2][:].rearrange("p (qt j) -> p qt j", qt=8),
                        in_=ffn_tm[0:1024, h * 128:(h + 1) * 128].rearrange("(qt p) j -> p qt j", p=128)),
                        inc=(eps_, 16))
                    ep_ld = p3.cnt(eps_)

                    p3.op("scalar", lambda g, kb=kb: g.activation(
                        ksq_sb[:], kb[:], AF.Square), waits=[(kvs, ld_h)], inc=("sKQ", 1))
                    ksq_done = p3.cnt("sKQ")
                    p3.op("vector", lambda g: g.memset(nrm[0:1, 0:2], -1.0e30),
                          waits=[("sNR", p3.cnt("sNR"))], inc=("sNR", 1))
                    red_cnt = {}
                    nm_base = p3.cnt("sNM")

                    def plan_norm(j, h=h):
                        kind, jj = ("k", j) if j < 8 else ("q", j - 8)
                        acc_col = 0 if kind == "k" else 1
                        pn = psN[j % 2]
                        w = [("sKQ", ksq_done)] if kind == "k" else [("sQE", qe_box[0])]
                        if j >= 2:
                            w.append(("sNR", red_cnt[j - 2]))
                        elif h >= 1:
                            w.append(("sBC", prev_bias_ready))
                            w.append(("sCE", 4 * h))
                        srcT = ksq_sb if kind == "k" else qsq_sb
                        incs = []
                        if 1 <= j <= 7 or j == 9:
                            incs.append(("sNM", 1))
                        p3.op("tensor", lambda g, pn=pn, srcT=srcT, jj=jj: g.matmul(
                            pn[0:1, :], ones_col_bf[:], srcT[:, jj * 512:(jj + 1) * 512],
                            start=True, stop=True),
                            waits=w, incs=incs)
                        p3.op("vector", lambda g, pn=pn: g.reduce_max(
                            nrm[0:1, 2:3], pn[0:1, :], axis=AX.X),
                            waits=[("sNM", nm_base + j + 1), ("sNR", p3.cnt("sNR"))], inc=("sNR", 1))
                        red_cnt[j] = p3.cnt("sNR")
                        p3.op("vector", lambda g, acc_col=acc_col: g.tensor_max(
                            nrm[0:1, acc_col:acc_col + 1], nrm[0:1, acc_col:acc_col + 1], nrm[0:1, 2:3]),
                            waits=[("sNR", p3.cnt("sNR"))], inc=("sNR", 1))

                    qe_box = [None]
                    for j in range(8):
                        plan_norm(j)
                    fence3("sNM")
                    w = [(kvs, ld_h)]
                    if h >= 1:
                        w.append(("sQE", 2 * h))
                        w.append(("sEX", exp_cnt[g_pair - 1]))
                    for ic in range(NC):
                        for nb in range(2):
                            first = (nb == 0 and ic == 0)
                            p3.op("tensor", lambda g, nb=nb, ic=ic, wb=wb: g.matmul(
                                ringA[:, nb * 512:(nb + 1) * 512],
                                wb[:, ic * 128:(ic + 1) * 128],
                                xn_own[:, ic * TOK + nb * 512: ic * TOK + (nb + 1) * 512],
                                start=(ic == 0), stop=(ic == NC - 1)),
                                waits=w if first else ())
                    fence3("sQP")
                    p3.op("scalar", lambda g, h=h: g.activation(
                        q_sb[:], ringA[:], AF.Identity, bias=bq_sb[:, h:h + 1], scale=1.0),
                        waits=[("sQP", h + 1)], inc=("sQE", 1))
                    p3.op("scalar", lambda g, h=h: g.activation(
                        qsq_sb[:], ringA[:], AF.Square, bias=bq_sb[:, h:h + 1], scale=1.0),
                        inc=("sQE", 1))
                    qe = p3.cnt("sQE")
                    qe_box[0] = qe
                    plan_norm(8)
                    plan_norm(9)
                    fence3("sNM")
                    p3.op("vector", lambda g: g.tensor_mul(nrm[0:1, 3:4], nrm[0:1, 0:1], nrm[0:1, 1:2]),
                          waits=[("sNR", p3.cnt("sNR"))], inc=("sNR", 1))
                    p3.op("scalar", lambda g: g.activation(
                        nrm[0:1, 4:5], nrm[0:1, 3:4], AF.Sqrt, bias=0.0, scale=1.0 / 128.0),
                        waits=[("sNR", p3.cnt("sNR"))], inc=("sBC", 1))
                    p3.op("vector", lambda g: g.tensor_scalar(
                        nrm[0:1, 5:6], nrm[0:1, 4:5], -1.0, None, ALU.mult),
                        waits=[("sBC", p3.cnt("sBC"))], inc=("sNR", 1))
                    p3.op("tensor", lambda g: g.matmul(
                        psE[:, 0:1], ones_row_f[:], nrm[0:1, 5:6], start=True, stop=True),
                        waits=[("sNR", p3.cnt("sNR"))])
                    p3.op("vector", lambda g: g.tensor_copy(bias_sb[:], psE[:, 0:1]),
                          waits=[("sSC", p3.cnt("sSC") + 1)], inc=("sBC", 1))
                    bias_ready = p3.cnt("sBC")
                    prev_bias_ready = bias_ready

                    for qh in range(2):
                        slots = [psE[:, 0:129], psE[:, 129:258], psE[:, 258:387], psF[:, 0:129]]
                        pair_base = g_pair

                        def plan_sc(tc, carries_fence_of=None, qh=qh, kb=kb, pair_base=pair_base):
                            gp_t = pair_base + tc // 2
                            ring = rings[gp_t % 3]
                            half = tc % 2
                            w = [("sQE", qe)]
                            if gp_t - 3 >= 0:
                                w.append(("sEX", exp_cnt[gp_t - 3]))
                            incs = []
                            if carries_fence_of is not None:
                                incs.append(("sSC", 1))
                            p3.op("tensor", lambda g, tc=tc, qh=qh, kb=kb, ring=ring, half=half: g.matmul(
                                ring[:, half * 512:(half + 1) * 512], kb[:, tc * 128:(tc + 1) * 128],
                                q_sb[:, qh * 512:(qh + 1) * 512], start=True, stop=True),
                                waits=w, incs=incs)
                            if carries_fence_of is not None:
                                fence_cnt[carries_fence_of] = p3.cnt("sSC")

                        plan_sc(0)
                        plan_sc(1)
                        plan_sc(2, carries_fence_of=pair_base)
                        plan_sc(3)
                        plan_sc(4, carries_fence_of=pair_base + 1)
                        plan_sc(5)
                        for lp in range(16):
                            gp = pair_base + lp
                            ring = rings[gp % 3]
                            p3.op("scalar", lambda g, gp=gp, ring=ring: g.activation(
                                pt_sb[gp % 3][:], ring[:, 0:1024], AF.Exp, bias=bias_sb[:], scale=RSCALE),
                                waits=[("sSC", fence_cnt[gp]), ("sBC", bias_ready)], inc=("sEX", 1))
                            exp_cnt[gp] = p3.cnt("sEX")
                            g_pair += 1
                            for u in range(2):
                                tcp = 2 * lp + u
                                exc = exp_cnt[gp]
                                for qb in range(4):
                                    w2 = ()
                                    incs = []
                                    if qb == 0:
                                        w2 = [("sEX", exc)]
                                        if tcp == 0 and n_ct >= 1:
                                            w2 = [("sEX", exc), ("sCE", n_ct * 2)]
                                        if u == 0 and lp + 2 <= 15:
                                            incs = [("sSC", 1)]
                                    st = (tcp == 0 and qb in (0, 3))
                                    stp = (tcp == NC - 1 and qb in (2, 3))
                                    p3.op("tensor", lambda g, slots=slots, qb=qb, tcp=tcp, vb=vb, gp=gp, u=u, st=st, stp=stp: g.matmul(
                                        slots[qb], pt_sb[gp % 3][:, u * 512 + qb * 128: u * 512 + (qb + 1) * 128],
                                        vb[:, tcp * 129:(tcp + 1) * 129], start=st, stop=stp),
                                        waits=w2, incs=incs)
                                    if incs:
                                        fence_cnt[pair_base + lp + 2] = p3.cnt("sSC")
                            if 2 * lp + 6 < NC:
                                plan_sc(2 * lp + 6)
                                plan_sc(2 * lp + 7)
                        n_ct += 1
                        fence3("sCT")
                        cpar = (n_ct - 1) % 2
                        w = [("sCT", n_ct)]
                        if n_ct >= 3:
                            w.append(("sEO", (n_ct - 2) * 4))
                        p3.op("scalar", lambda g, cpar=cpar: g.activation(
                            ctx_sb[cpar][:, 0:387], psE[:, 0:387], AF.Copy), waits=w, inc=("sCE", 1))
                        p3.op("scalar", lambda g, cpar=cpar: g.activation(
                            ctx_sb[cpar][:, 387:516], psF[:, 0:129], AF.Copy), inc=("sCE", 1))
                        ce = p3.cnt("sCE")
                        for qb in range(4):
                            qt = qh * 4 + qb
                            r0 = qt * 128
                            par2 = n_eo % 2
                            cb = qb * 129
                            c0_ = qt * 128
                            p3.op("vector", lambda g, par2=par2, c0_=c0_, h=h: g.tensor_add(
                                xf_t[par2][:], x_t[h % 2][:, c0_:c0_ + 128], f_t[h % 2][:, c0_:c0_ + 128]),
                                waits=[(eps_, ep_ld), ("sEO", p3.cnt("sEO") - 1)] if n_eo >= 2
                                      else [(eps_, ep_ld)],
                                inc=("sNR", 1))
                            p3.op("vector", lambda g, cpar=cpar, cb=cb: g.reciprocal(
                                rden[:], ctx_sb[cpar][:, cb + 128:cb + 129]),
                                waits=[("sCE", ce), ("sNR", p3.cnt("sNR"))], inc=("sNR", 1))
                            dos = f"dOut{par2}"
                            stt_w = [("sNR", p3.cnt("sNR"))]
                            if n_eo >= 2:
                                stt_w.append((dos, p3.cnt(dos)))
                            p3.op("vector", lambda g, cpar=cpar, cb=cb, par2=par2: g.scalar_tensor_tensor(
                                o_sb[par2][:], ctx_sb[cpar][:, cb:cb + 128], rden[:, 0:1], xf_t[par2][:],
                                ALU.mult, ALU.add),
                                waits=stt_w, inc=("sEO", 1))
                            eo = p3.cnt("sEO")
                            n_eo += 1
                            p3.op("gpsimd", lambda g, par2=par2, r0=r0, h=h: g.dma_start(
                                out=out_d[r0:r0 + 128, h * 128:(h + 1) * 128], in_=o_sb[par2][:]),
                                waits=[("sEO", eo)], inc=(dos, 16))

                p3.op("gpsimd", lambda g: g.nop(), waits=[
                    ("dOut0", p3.cnt("dOut0")), ("dOut1", p3.cnt("dOut1")),
                    ("dEp0", p3.cnt("dEp0")), ("dEp1", p3.cnt("dEp1"))])
                p3.op("sync", lambda g: g.nop(), waits=[("dKVi0", p3.cnt("dKVi0")), ("dKVi1", p3.cnt("dKVi1"))])
                barrier(p3, ringB[0:128, 0:128])
                emit(nc, p3, sems)

    return nc


def _prep_inputs(inp):
    f32 = np.float32
    x = np.asarray(inp["L0_attn_hidden_states"], f32)
    wq = np.asarray(inp["L0_attn_wq"], f32)
    bq = np.asarray(inp["L0_attn_bq"], f32)
    wk = np.asarray(inp["L0_attn_wk"], f32)
    bk = np.asarray(inp["L0_attn_bk"], f32)
    wv = np.asarray(inp["L0_attn_wv"], f32)
    bv = np.asarray(inp["L0_attn_bv"], f32)
    gw = np.asarray(inp["L0_ffn_gate_w"])
    uw = np.asarray(inp["L0_ffn_up_w"])
    dw = np.asarray(inp["L0_ffn_down_w"])

    def chunk_oc(wT):
        a = wT.reshape(NC, 128, NC, 128).transpose(2, 1, 0, 3).reshape(NC, 128, D)
        return np.ascontiguousarray(a).astype(bfloat16)

    def chunk_f(wT):
        a = wT.reshape(NC, 128, NF, 128).transpose(2, 1, 0, 3).reshape(NF, 128, D)
        return np.ascontiguousarray(a).astype(bfloat16)

    com = {
        "wq": chunk_oc(wq.T),
        "wk": chunk_oc(wk.T),
        "wv": np.ascontiguousarray(wv.T).astype(bfloat16),
        "bq": np.ascontiguousarray(bq.reshape(NH, 128).T).astype(f32),
        "bk": np.ascontiguousarray(bk.reshape(NC, 128).T).astype(f32),
        "bv": bv.astype(bfloat16).reshape(1, D),
        "gw": chunk_f(gw.T.astype(f32)),
        "uw": chunk_f(uw.T.astype(f32)),
        "dw": np.ascontiguousarray(
            dw.T.astype(f32).reshape(NF, 128, 8, 512).transpose(2, 0, 1, 3)).astype(bfloat16),
        "gs": np.asarray(inp["L0_ffn_gate_s"], f32).reshape(1, 1),
        "us": np.asarray(inp["L0_ffn_up_s"], f32).reshape(1, 1),
        "ds": np.asarray(inp["L0_ffn_down_s"], f32).reshape(1, 1),
    }
    in_maps = []
    for c in range(8):
        b, qr = c // 4, c % 4
        m = dict(com)
        m["x"] = np.ascontiguousarray(np.roll(x[b], -qr * TOK, axis=0))
        in_maps.append(m)
    return in_maps


def kernel(**inp):
    in_maps = _prep_inputs(inp)
    nc = build(debug=False)
    res = run_bass_kernel_spmd(nc, in_maps, core_ids=list(range(8)))
    out = np.empty((2, S, D), dtype=np.float32)
    for c in range(8):
        b, qr = c // 4, c % 4
        out[b, qr * TOK:(qr + 1) * TOK, :] = res.results[c]["out"]
    return out
